# version 01
# baseline (speedup 1.0000x reference)
import numpy as np
import ml_dtypes
from contextlib import ExitStack

import concourse.bass as bass
import concourse.mybir as mybir
from concourse.bass_utils import run_bass_kernel_spmd

F32 = mybir.dt.float32
BF16 = mybir.dt.bfloat16
I32 = mybir.dt.int32
U16 = mybir.dt.uint16
AF = mybir.ActivationFunctionType
ALU = mybir.AluOpType
AX = mybir.AxisListType

NCORE = 8
TOK = 2048
D = 2048
NTT = TOK // 128
NKC = D // 128
INV_SQRT_HD = 0.08838834764831845

bf = ml_dtypes.bfloat16

_prog_cache = {}



def build_qkv(rope: bool, bias: bool, debug: bool = False):
    nc = bass.Bass("TRN2", target_bir_lowering=False)
    x_d = nc.dram_tensor("x", [TOK, D], F32, kind="ExternalInput").ap()
    wT_d = nc.dram_tensor("wT", [D, 3 * D], F32, kind="ExternalInput").ap()
    if bias:
        b_d = nc.dram_tensor("b", [1, 3 * D], F32, kind="ExternalInput").ap()
    if rope:
        cos_d = nc.dram_tensor("cos", [TOK, 64], F32, kind="ExternalInput").ap()
        sin_d = nc.dram_tensor("sin", [TOK, 64], F32, kind="ExternalInput").ap()
    outs = [nc.dram_tensor(nm, [TOK, D], BF16, kind="ExternalOutput").ap()
            for nm in ("q", "k", "v")]
    nc._test_outs = {"q": outs[0], "k": outs[1], "v": outs[2]}
    if debug:
        dbg_yT = nc.dram_tensor("dbg_yT", [128, NKC * TOK], BF16, kind="ExternalOutput").ap()
        dbg_st = nc.dram_tensor("dbg_st", [128, 6 * NTT], F32, kind="ExternalOutput").ap()

    ident = nc.inline_tensor(np.eye(128, dtype=bf), "ident128")
    ones = nc.inline_tensor(np.ones((1, 128), dtype=bf), "ones_row")

    with ExitStack() as st_:
        def sb(name, shape, dt):
            return st_.enter_context(nc.sbuf_tensor(name, shape, dt))
        xt = [sb(f"xt{i}", [128, D], F32) for i in range(2)]
        sqt = sb("sqt", [128, D], F32)
        yt = [sb(f"yt{i}", [128, D], BF16) for i in range(2)]
        sums = sb("sums", [128, NTT], F32)
        sumsq = sb("sumsq", [128, NTT], F32)
        mneg = sb("mneg", [128, NTT], F32)
        varp = sb("varp", [128, NTT], F32)
        vinv = sb("vinv", [128, NTT], F32)
        rstd = sb("rstd", [128, NTT], F32)
        yT = sb("yT", [128, NKC * TOK], BF16)
        wsb = sb("wsb", [128, NKC * D], BF16)
        id_sb = sb("id_sb", [128, 128], BF16)
        ones_sb = sb("ones_sb", [1, 128], BF16)
        brow = sb("brow", [1, 3 * D], BF16)
        otile = [sb(f"otile{i}", [128, D], BF16) for i in range(2)]
        rtile = [sb(f"rtile{i}", [128, D], BF16) for i in range(2)]
        rtmp = sb("rtmp", [128, D], BF16)
        cos_sb = sb("cos_sb", [128, NTT * 64], F32)
        sin_sb = sb("sin_sb", [128, NTT * 64], F32)

        psm = [st_.enter_context(nc.psum_tensor(f"psm{i}", [128, D], F32)) for i in range(2)]
        pst = [psm[i][:, 0:D // 2].bitcast(BF16) for i in range(2)]

        s_c = st_.enter_context(nc.semaphore("s_c"))
        s_g = st_.enter_context(nc.semaphore("s_g"))
        s_x = [st_.enter_context(nc.semaphore(f"s_x{i}")) for i in range(2)]
        s_so = [st_.enter_context(nc.semaphore(f"s_so{i}")) for i in range(2)]
        s_dve = st_.enter_context(nc.semaphore("s_dve"))
        s_act = st_.enter_context(nc.semaphore("s_act"))
        s_pe = st_.enter_context(nc.semaphore("s_pe"))
        s_ev = st_.enter_context(nc.semaphore("s_ev"))
        s_rcp = st_.enter_context(nc.semaphore("s_rcp"))

        nconst = 16 * (2 + (2 if rope else 0))
        with nc.Block() as blk:
            @blk.sync
            def _(sync):
                sync.dma_start(out=id_sb[:], in_=ident.ap()).then_inc(s_c, 16)
                sync.dma_start(out=ones_sb[:], in_=ones.ap()).then_inc(s_c, 16)
                if rope:
                    sync.dma_start(out=cos_sb[:].rearrange("p (t e) -> p t e", t=NTT),
                                   in_=cos_d.rearrange("(t p) e -> p t e", p=128)).then_inc(s_c, 16)
                    sync.dma_start(out=sin_sb[:].rearrange("p (t e) -> p t e", t=NTT),
                                   in_=sin_d.rearrange("(t p) e -> p t e", p=128)).then_inc(s_c, 16)
                sync.wait_ge(s_c, nconst)
            @blk.gpsimd
            def _(g):
                if bias:
                    g.dma_start(out=brow[:], in_=b_d).then_inc(s_g, 16)
                    g.wait_ge(s_g, 16)
                else:
                    g.memset(brow[:], 0.0)
        g_cnt = 16 if bias else 0
        nc.all_engine_barrier()

        with nc.Block() as blk:
            @blk.sync
            def _(sync):
                for tt in range(NTT):
                    if tt >= 2:
                        sync.wait_ge(s_dve, 2 * (tt - 1))
                    sync.dma_start(out=xt[tt % 2][:],
                                   in_=x_d[tt * 128:(tt + 1) * 128, :]).then_inc(s_x[tt % 2], 16)

            @blk.scalar
            def _(s):
                for tt in range(NTT):
                    s.wait_ge(s_x[tt % 2], 16 * (tt // 2 + 1))
                    s.activation(sqt[:], xt[tt % 2][:], AF.Square,
                                 accum_out=sumsq[:, tt:tt + 1]).then_inc(s_act, 1)
                    s.wait_ge(s_dve, 2 * tt + 1)
                    s.activation(rstd[:, tt:tt + 1], vinv[:, tt:tt + 1], AF.Sqrt).then_inc(s_act, 1)

            @blk.vector
            def _(v):
                rc = [0]
                def ss(ins):
                    rc[0] += 1
                    ins.then_inc(s_rcp, 1)
                    v.wait_ge(s_rcp, rc[0])
                for tt in range(NTT):
                    v.wait_ge(s_x[tt % 2], 16 * (tt // 2 + 1))
                    ss(v.reduce_sum(sums[:, tt:tt + 1], xt[tt % 2][:], axis=AX.X))
                    v.wait_ge(s_act, 2 * tt + 1)
                    ss(v.tensor_scalar(mneg[:, tt:tt + 1], sums[:, tt:tt + 1], -1.0 / D, None, ALU.mult))
                    ss(v.tensor_tensor(varp[:, tt:tt + 1], mneg[:, tt:tt + 1], mneg[:, tt:tt + 1], ALU.mult))
                    ss(v.scalar_tensor_tensor(varp[:, tt:tt + 1], sumsq[:, tt:tt + 1], 1.0 / D,
                                              varp[:, tt:tt + 1], ALU.mult, ALU.subtract))
                    ss(v.tensor_scalar(rstd[:, tt:tt + 1], varp[:, tt:tt + 1], 1e-5, None, ALU.add))
                    ss(v.reciprocal(vinv[:, tt:tt + 1], rstd[:, tt:tt + 1]))
                    v.engine_nop().then_inc(s_dve, 1)
                    v.wait_ge(s_act, 2 * tt + 2)
                    if tt >= 2:
                        v.wait_ge(s_pe, tt - 1)
                    v.tensor_scalar(yt[tt % 2][:], xt[tt % 2][:], mneg[:, tt:tt + 1],
                                    rstd[:, tt:tt + 1], ALU.add, ALU.mult).then_inc(s_dve, 1)
                    v.wait_ge(s_pe, tt + 1)
                    v.tensor_copy(
                        bass.AP(yT, tt * 128, [[NKC * TOK, 128], [TOK, NKC], [1, 128]]),
                        pst[tt % 2].rearrange("p (c e) -> p c e", c=NKC),
                    ).then_inc(s_ev, 1)

            @blk.tensor
            def _(t):
                for tt in range(NTT):
                    t.wait_ge(s_dve, 2 * tt + 2)
                    if tt >= 2:
                        t.wait_ge(s_ev, tt - 1)
                    for c in range(NKC):
                        ins = t.transpose(pst[tt % 2][:, c * 128:(c + 1) * 128],
                                          yt[tt % 2][:, c * 128:(c + 1) * 128], id_sb[:])
                    ins.then_inc(s_pe, 1)
        nc.all_engine_barrier()
        if debug:
            with nc.Block() as blk:
                @blk.sync
                def _(sync):
                    sync.dma_start(out=dbg_yT, in_=yT[:]).then_inc(s_c, 16)
                    for i, buf in enumerate((sums, sumsq, mneg, varp, vinv, rstd)):
                        sync.dma_start(out=dbg_st[:, i * NTT:(i + 1) * NTT], in_=buf[:]).then_inc(s_c, 16)
                    sync.wait_ge(s_c, nconst + 16 * 7)
            nc.all_engine_barrier()

        for third in range(3):
            do_rope = rope and third < 2
            pe_base = NTT + third * NTT
            act_base = 2 * NTT + third * NTT
            dve_base = 2 * NTT + sum(1 for th in range(third) if rope and th < 2) * NTT

            with nc.Block() as blk:
                @blk.gpsimd
                def _(g):
                    for k in range(NKC):
                        g.dma_start(out=wsb[:, k * D:(k + 1) * D],
                                    in_=wT_d[k * 128:(k + 1) * 128, third * D:(third + 1) * D]
                                    ).then_inc(s_g, 16)
            g_cnt += 16 * NKC
            ldw = g_cnt
            with nc.Block() as blk:
                @blk.vector
                def _(v):
                    v.wait_ge(s_g, ldw)
            nc.all_engine_barrier()

            with nc.Block() as blk:
                @blk.tensor
                def _(t):
                    for tt in range(NTT):
                        if tt >= 2:
                            t.wait_ge(s_act, act_base + tt - 1)
                        pp = psm[tt % 2]
                        for k in range(NKC):
                            for oc in range(4):
                                t.matmul(pp[:, oc * 512:(oc + 1) * 512],
                                         yT[:, k * TOK + tt * 128: k * TOK + (tt + 1) * 128],
                                         wsb[:, k * D + oc * 512: k * D + (oc + 1) * 512],
                                         start=(k == 0), stop=False)
                        for oc in range(4):
                            ins = t.matmul(pp[:, oc * 512:(oc + 1) * 512],
                                           ones_sb[:],
                                           brow[0:1, third * D + oc * 512: third * D + (oc + 1) * 512],
                                           start=False, stop=True)
                        ins.then_inc(s_pe, 1)

                @blk.scalar
                def _(s):
                    for tt in range(NTT):
                        s.wait_ge(s_pe, pe_base + tt + 1)
                        if tt >= 2:
                            if do_rope:
                                s.wait_ge(s_dve, dve_base + tt - 1)
                            else:
                                s.wait_ge(s_so[tt % 2], 16 * ((third * NTT + tt - 2) // 2 + 1))
                        s.activation(otile[tt % 2][:], psm[tt % 2][:], AF.Copy).then_inc(s_act, 1)

                if do_rope:
                    @blk.vector
                    def _(v):
                        for tt in range(NTT):
                            v.wait_ge(s_act, act_base + tt + 1)
                            if tt >= 2:
                                v.wait_ge(s_so[tt % 2], 16 * ((third * NTT + tt - 2) // 2 + 1))
                            ot = otile[tt % 2]
                            rt = rtile[tt % 2]
                            hA = lambda buf: bass.AP(buf, 0, [[D, 128], [128, 16], [1, 64]])
                            hB = lambda buf: bass.AP(buf, 64, [[D, 128], [128, 16], [1, 64]])
                            cosr = bass.AP(cos_sb, tt * 64, [[NTT * 64, 128], [0, 16], [1, 64]])
                            sinr = bass.AP(sin_sb, tt * 64, [[NTT * 64, 128], [0, 16], [1, 64]])
                            v.tensor_tensor(hA(rt), hA(ot), cosr, ALU.mult)
                            v.tensor_tensor(hA(rtmp), hB(ot), sinr, ALU.mult)
                            v.tensor_tensor(hA(rt), hA(rt), hA(rtmp), ALU.subtract)
                            v.tensor_tensor(hB(rt), hB(ot), cosr, ALU.mult)
                            v.tensor_tensor(hB(rtmp), hA(ot), sinr, ALU.mult)
                            v.tensor_tensor(hB(rt), hB(rt), hB(rtmp), ALU.add).then_inc(s_dve, 1)

                @blk.sync
                def _(sync):
                    for tt in range(NTT):
                        gi = third * NTT + tt
                        if do_rope:
                            sync.wait_ge(s_dve, dve_base + tt + 1)
                            src = rtile[tt % 2]
                        else:
                            sync.wait_ge(s_act, act_base + tt + 1)
                            src = otile[tt % 2]
                        if gi >= 2:
                            sync.wait_ge(s_so[gi % 2], 16 * ((gi - 2) // 2 + 1))
                        sync.dma_start(out=outs[third][tt * 128:(tt + 1) * 128, :],
                                       in_=src[:]).then_inc(s_so[gi % 2], 16)
            nc.all_engine_barrier()

        with nc.Block() as blk:
            @blk.sync
            def _(sync):
                sync.wait_ge(s_so[0], 16 * ((3 * NTT + 1) // 2))
                sync.wait_ge(s_so[1], 16 * (3 * NTT // 2))
    return nc



def build_proj(bias: bool):
    nc = bass.Bass("TRN2", target_bir_lowering=False)
    aT_d = nc.dram_tensor("aT", [D, TOK], BF16, kind="ExternalInput").ap()
    wT_d = nc.dram_tensor("wT", [D, D], F32, kind="ExternalInput").ap()
    if bias:
        b_d = nc.dram_tensor("b", [1, D], F32, kind="ExternalInput").ap()
    x_d = nc.dram_tensor("x", [TOK, D], F32, kind="ExternalInput").ap()
    xo_d = nc.dram_tensor("xo", [TOK, D], F32, kind="ExternalOutput").ap()
    nc._test_outs = {"xo": xo_d}
    ones = nc.inline_tensor(np.ones((1, 128), dtype=bf), "ones_row")

    with ExitStack() as st_:
        def sb(name, shape, dt):
            return st_.enter_context(nc.sbuf_tensor(name, shape, dt))
        aT_sb = sb("aT_sb", [128, NKC * TOK], BF16)
        wsb = sb("wsb", [128, NKC * D], BF16)
        brow = sb("brow", [1, D], BF16)
        ones_sb = sb("ones_sb", [1, 128], BF16)
        xt = [sb(f"xt{i}", [128, D], F32) for i in range(2)]
        ot = [sb(f"ot{i}", [128, D], F32) for i in range(2)]
        psm = [st_.enter_context(nc.psum_tensor(f"psm{i}", [128, D], F32)) for i in range(2)]
        s_c = st_.enter_context(nc.semaphore("s_c"))
        s_g = st_.enter_context(nc.semaphore("s_g"))
        s_x = [st_.enter_context(nc.semaphore(f"s_x{i}")) for i in range(2)]
        s_so = [st_.enter_context(nc.semaphore(f"s_so{i}")) for i in range(2)]
        s_pe = st_.enter_context(nc.semaphore("s_pe"))
        s_dve = st_.enter_context(nc.semaphore("s_dve"))

        with nc.Block() as blk:
            @blk.sync
            def _(sync):
                for k in range(NKC):
                    sync.dma_start(out=aT_sb[:, k * TOK:(k + 1) * TOK],
                                   in_=aT_d[k * 128:(k + 1) * 128, :]).then_inc(s_c, 16)
                sync.dma_start(out=ones_sb[:], in_=ones.ap()).then_inc(s_c, 16)
                sync.wait_ge(s_c, 16 * (NKC + 1))
            @blk.gpsimd
            def _(g):
                for k in range(NKC):
                    g.dma_start(out=wsb[:, k * D:(k + 1) * D],
                                in_=wT_d[k * 128:(k + 1) * 128, :]).then_inc(s_g, 16)
                if bias:
                    g.dma_start(out=brow[:], in_=b_d).then_inc(s_g, 16)
                else:
                    g.memset(brow[:], 0.0)
                g.wait_ge(s_g, 16 * (NKC + (1 if bias else 0)))
        nc.all_engine_barrier()

        with nc.Block() as blk:
            @blk.sync
            def _(sync):
                for tt in range(NTT):
                    if tt >= 2:
                        sync.wait_ge(s_dve, tt - 1)
                    sync.dma_start(out=xt[tt % 2][:],
                                   in_=x_d[tt * 128:(tt + 1) * 128, :]).then_inc(s_x[tt % 2], 16)

            @blk.tensor
            def _(t):
                for tt in range(NTT):
                    if tt >= 2:
                        t.wait_ge(s_dve, tt - 1)
                    pp = psm[tt % 2]
                    for k in range(NKC):
                        for oc in range(4):
                            t.matmul(pp[:, oc * 512:(oc + 1) * 512],
                                     aT_sb[:, k * TOK + tt * 128: k * TOK + (tt + 1) * 128],
                                     wsb[:, k * D + oc * 512: k * D + (oc + 1) * 512],
                                     start=(k == 0), stop=False)
                    for oc in range(4):
                        ins = t.matmul(pp[:, oc * 512:(oc + 1) * 512], ones_sb[:],
                                       brow[0:1, oc * 512:(oc + 1) * 512],
                                       start=False, stop=True)
                    ins.then_inc(s_pe, 1)

            @blk.vector
            def _(v):
                for tt in range(NTT):
                    v.wait_ge(s_pe, tt + 1)
                    v.wait_ge(s_x[tt % 2], 16 * (tt // 2 + 1))
                    if tt >= 2:
                        v.wait_ge(s_so[tt % 2], 16 * ((tt - 2) // 2 + 1))
                    v.tensor_tensor(ot[tt % 2][:], psm[tt % 2][:], xt[tt % 2][:],
                                    ALU.add).then_inc(s_dve, 1)

            @blk.gpsimd
            def _(g):
                for tt in range(NTT):
                    g.wait_ge(s_dve, tt + 1)
                    g.dma_start(out=xo_d[tt * 128:(tt + 1) * 128, :],
                                in_=ot[tt % 2][:]).then_inc(s_so[tt % 2], 16)
        nc.all_engine_barrier()
        with nc.Block() as blk:
            @blk.sync
            def _(sync):
                sync.wait_ge(s_so[0], 16 * (NTT // 2))
                sync.wait_ge(s_so[1], 16 * (NTT // 2))
    return nc



def _kt_list(l1, qc):
    if l1:
        return list(range(0, 4)) + list(range(16, 32))
    return list(range(0, qc * 8 + 8))


def build_attn(l1: bool, nbh: int = 8):
    nc = bass.Bass("TRN2", target_bir_lowering=False)
    qT_d = nc.dram_tensor("qT", [1024, 4096], BF16, kind="ExternalInput").ap()
    kT_d = nc.dram_tensor("kT", [1024, 4096], BF16, kind="ExternalInput").ap()
    vv_d = nc.dram_tensor("vv", [8 * 4096, 128], BF16, kind="ExternalInput").ap()
    if l1:
        bu_d = nc.dram_tensor("bu", [1, 8], F32, kind="ExternalInput").ap()
    aT_d = nc.dram_tensor("aT", [1024, 4096], BF16, kind="ExternalOutput").ap()
    nc._test_outs = {"aT": aT_d}

    tri_np = (np.arange(128)[None, :] >= np.arange(128)[:, None]).astype(bf)
    tri = nc.inline_tensor(tri_np, "tri")
    cm3_np = (np.arange(128)[:, None] < 25).astype(np.float32)
    cm3 = nc.inline_tensor(cm3_np, "cm3")
    onc = nc.inline_tensor(np.ones((128, 1), np.float32), "onc")
    onr = nc.inline_tensor(np.ones((1, 128), np.float32), "onr")

    NQC = 4
    with ExitStack() as st_:
        def sb(name, shape, dt):
            return st_.enter_context(nc.sbuf_tensor(name, shape, dt))
        q_sb = sb("q_sb", [128, 4096], BF16)
        k_sb = sb("k_sb", [128, 4096], BF16)
        v_sb = sb("v_sb", [128, 4096], BF16)
        et = [sb(f"et{i}", [128, 1024], BF16) for i in range(2)]
        acc = sb("acc", [128, 1024], F32)
        accb = sb("accb", [128, 1024], F32)
        rb_sb = sb("rb_sb", [128, 1024], F32)
        att = [sb(f"att{i}", [128, 1024], BF16) for i in range(2)]
        tri_sb = sb("tri_sb", [128, 128], BF16)
        cm3_sb = sb("cm3_sb", [128, 1], F32)
        onc_sb = sb("onc_sb", [128, 1], F32)
        onr_sb = sb("onr_sb", [1, 128], F32)
        eb_sb = sb("eb_sb", [128, 20], F32)
        prep_sb = sb("prep_sb", [128, 2], F32)
        tot_sb = sb("tot_sb", [1, 2], F32)
        e2v = sb("e2v", [1, 1], F32)
        sc_sb = sb("sc_sb", [1, 8], F32)
        e2r = sb("e2r", [1, 4096], F32)
        s2u = sb("s2u", [1, 1024], F32)
        s2ubf = sb("s2ubf", [1, 1024], BF16)
        ve_sb = sb("ve_sb", [1, 128], BF16)
        bu_sb = sb("bu_sb", [1, 8], F32)
        rr = sb("rr", [1, 1024], F32)

        ps_s = [st_.enter_context(nc.psum_tensor(f"ps_s{i}", [128, 1024], F32)) for i in range(2)]
        ps_pv = st_.enter_context(nc.psum_tensor("ps_pv", [128, 1024], F32))
        nc._test_sb = dict(acc=acc, accb=accb, et0=et[0], et1=et[1], q=q_sb, k=k_sb,
                           v=v_sb, rr=rr, rb=rb_sb, ps_s0=ps_s[0], ps_s1=ps_s[1], ps_pv=ps_pv)

        s_ld = st_.enter_context(nc.semaphore("s_ld"))
        s_so = [st_.enter_context(nc.semaphore(f"s_so{i}")) for i in range(2)]
        s_pe = st_.enter_context(nc.semaphore("s_pe"))
        s_pv = st_.enter_context(nc.semaphore("s_pv"))
        s_pd = st_.enter_context(nc.semaphore("s_pd"))
        s_act = st_.enter_context(nc.semaphore("s_act"))
        s_msk = st_.enter_context(nc.semaphore("s_msk"))
        s_gp = st_.enter_context(nc.semaphore("s_gp"))
        s_dve = st_.enter_context(nc.semaphore("s_dve"))
        s_ss = st_.enter_context(nc.semaphore("s_ss"))

        with nc.Block() as blk:
            @blk.sync
            def _(sync):
                sync.dma_start(out=tri_sb[:], in_=tri.ap()).then_inc(s_ld, 16)
                sync.dma_start(out=cm3_sb[:], in_=cm3.ap()).then_inc(s_ld, 16)
                sync.dma_start(out=onc_sb[:], in_=onc.ap()).then_inc(s_ld, 16)
                sync.dma_start(out=onr_sb[:], in_=onr.ap()).then_inc(s_ld, 16)
                if l1:
                    sync.dma_start(out=bu_sb[:], in_=bu_d).then_inc(s_ld, 16)
                sync.wait_ge(s_ld, 16 * (4 + (1 if l1 else 0)))
        nc.all_engine_barrier()
        with nc.Block() as blk:
            @blk.vector
            def _(v):
                v.sem_clear(s_ld)
        nc.all_engine_barrier()

        klists = [_kt_list(l1, qc) for qc in range(NQC)]
        nks = [len(kl) for kl in klists]
        gbase = [sum(nks[:qc]) for qc in range(NQC)]
        pre_act = 1 if l1 else 0
        pre_dve = 2 if l1 else 0
        pre_pd = 2 if l1 else 0
        ndve_qc = 3 if l1 else 2
        npd_qc = 3 if l1 else 2

        def g_of(qc, ki):
            return gbase[qc] + ki

        def qcki_of(g):
            for qc in range(NQC):
                if gbase[qc] <= g < gbase[qc] + nks[qc]:
                    return qc, g - gbase[qc]
            raise AssertionError

        def j0f(qc, K):
            return 0 if l1 else max(0, K * 128 - qc * 1024)

        def jff(qc, K):
            return (j0f(qc, K) // 512) * 512

        def expd(g):
            return pre_act + g + 1

        def mskd(g):
            return g + 1

        def accd(qc, ki):
            return gbase[qc] + qc + 1 + ki + 1

        def pvr(qc):
            return sum(nks[q] + (1 if l1 else 0) for q in range(qc))

        def pvd(qc, ki):
            return pvr(qc) + ki + 1

        def dved(qc, i):
            return pre_dve + ndve_qc * qc + i

        def pdd(qc, i):
            return pre_pd + npd_qc * qc + i

        gso_n = [0]

        for bh in range(nbh):
            with nc.Block() as blk:
                @blk.sync
                def _(sync):
                    sync.dma_start(out=q_sb[:], in_=qT_d[bh * 128:(bh + 1) * 128, :]).then_inc(s_ld, 16)
                    sync.dma_start(out=k_sb[:], in_=kT_d[bh * 128:(bh + 1) * 128, :]).then_inc(s_ld, 16)
                    vsrc = bass.AP(vv_d.tensor, bh * 4096 * 128,
                                   [[128, 128], [128 * 128, 32], [1, 128]])
                    sync.dma_start(out=v_sb[:].rearrange("p (K e) -> p K e", K=32),
                                   in_=vsrc).then_inc(s_ld, 16)
                    sync.wait_ge(s_ld, 48)
            nc.all_engine_barrier()

            with nc.Block() as blk:
                @blk.tensor
                def _(t):
                    def qk(qc, ki):
                        K = klists[qc][ki]
                        g = g_of(qc, ki)
                        jf = jff(qc, K)
                        if g >= 2:
                            t.wait_ge(s_act, expd(g - 2))
                        if qc > 0 and ki < 2:
                            t.wait_ge(s_dve, dved(qc - 1, ndve_qc))
                        for sc in range(jf // 512, 2):
                            ins = t.matmul(ps_s[g % 2][:, sc * 512:(sc + 1) * 512],
                                           k_sb[:, K * 128:(K + 1) * 128],
                                           q_sb[:, qc * 1024 + sc * 512: qc * 1024 + (sc + 1) * 512],
                                           start=True, stop=True)
                        ins.then_inc(s_pe, 1)

                    def pv(qc, ki):
                        K = klists[qc][ki]
                        g = g_of(qc, ki)
                        jf = jff(qc, K)
                        t.wait_ge(s_msk, mskd(g))
                        kl = klists[qc]
                        for sc in range(jf // 512, 2):
                            touch = [kk for kk in kl if jff(qc, kk) <= sc * 512]
                            stop = (K == touch[-1]) and not l1
                            ins = t.matmul(ps_pv[:, sc * 512:(sc + 1) * 512],
                                           v_sb[:, K * 128:(K + 1) * 128],
                                           et[g % 2][:, sc * 512:(sc + 1) * 512],
                                           start=(ki == 0), stop=stop)
                        ins.then_inc(s_pv, 1)

                    if l1:
                        for idx, K in enumerate(klists[0]):
                            ins = t.matmul(ps_pv[:, idx:idx + 1],
                                           k_sb[:, K * 128:(K + 1) * 128],
                                           q_sb[:, 4095:4096], start=True, stop=True)
                        ins.then_inc(s_pd, 1)
                        t.wait_ge(s_dve, 1)
                        ins = t.matmul(ps_s[1][0:1, 0:2], onc_sb[:], prep_sb[:],
                                       start=True, stop=True)
                        ins.then_inc(s_pd, 1)

                    for qc in range(NQC):
                        nk = nks[qc]
                        for ki in range(nk):
                            qk(qc, ki)
                            if ki >= 1:
                                pv(qc, ki - 1)
                        pv(qc, nk - 1)
                        t.wait_ge(s_gp, accd(qc, nk - 1))
                        for sc in range(2):
                            ins = t.matmul(ps_s[1][0:1, sc * 512:(sc + 1) * 512],
                                           onc_sb[:], acc[:, sc * 512:(sc + 1) * 512],
                                           start=True, stop=not l1)
                            if l1:
                                ins = t.matmul(ps_s[1][0:1, sc * 512:(sc + 1) * 512],
                                               onc_sb[:], accb[:, sc * 512:(sc + 1) * 512],
                                               start=False, stop=True)
                        ins.then_inc(s_pd, 1)
                        if l1:
                            for sc in range(2):
                                ins = t.matmul(ps_s[0][0:1, sc * 512:(sc + 1) * 512],
                                               onc_sb[:], accb[:, sc * 512:(sc + 1) * 512],
                                               start=True, stop=True)
                            ins.then_inc(s_pd, 1)
                            t.wait_ge(s_dve, dved(qc, 2))
                            for sc in range(2):
                                ins = t.matmul(ps_pv[:, sc * 512:(sc + 1) * 512],
                                               ve_sb[:], s2ubf[0:1, sc * 512:(sc + 1) * 512],
                                               start=False, stop=True)
                            ins.then_inc(s_pv, 1)
                        else:
                            t.wait_ge(s_dve, dved(qc, 1))
                        if l1:
                            pass
                        for sc in range(2):
                            ins = t.matmul(ps_s[0][:, sc * 512:(sc + 1) * 512],
                                           onr_sb[:], rr[0:1, sc * 512:(sc + 1) * 512],
                                           start=True, stop=True)
                        ins.then_inc(s_pd, 1)

                @blk.scalar
                def _(s):
                    if l1:
                        s.wait_ge(s_pd, 1)
                        s.activation(eb_sb[:], ps_pv[:, 0:20], AF.Exp,
                                     scale=INV_SQRT_HD).then_inc(s_act, 1)
                    for qc in range(NQC):
                        for ki, K in enumerate(klists[qc]):
                            g = g_of(qc, ki)
                            jf = jff(qc, K)
                            s.wait_ge(s_pe, g + 1)
                            if g >= 2:
                                q2, k2 = qcki_of(g - 2)
                                s.wait_ge(s_pv, pvd(q2, k2))
                                s.wait_ge(s_gp, accd(q2, k2))
                            s.activation(et[g % 2][:, jf:1024], ps_s[g % 2][:, jf:1024],
                                         AF.Exp, scale=INV_SQRT_HD).then_inc(s_act, 1)

                @blk.vector
                def _(v):
                    ssn = [0]
                    def ss(ins):
                        ssn[0] += 1
                        ins.then_inc(s_ss, 1)
                        v.wait_ge(s_ss, ssn[0])
                    if l1:
                        v.wait_ge(s_act, 1)
                        ss(v.tensor_scalar(eb_sb[:, 3:4], eb_sb[:, 3:4],
                                           cm3_sb[:, 0:1], None, ALU.mult))
                        ss(v.reduce_sum(prep_sb[:, 0:1], eb_sb[:, 0:20], axis=AX.X))
                        ss(v.reduce_sum(prep_sb[:, 1:2], eb_sb[:, 4:20], axis=AX.X))
                        v.engine_nop().then_inc(s_dve, 1)
                        v.wait_ge(s_pd, 2)
                        ss(v.tensor_copy(tot_sb[:], ps_s[1][0:1, 0:2]))
                        ss(v.tensor_copy(e2v[:], eb_sb[0:1, 4:5]))
                        ss(v.tensor_tensor(sc_sb[0:1, 0:1], tot_sb[0:1, 1:2], e2v[:], ALU.subtract))
                        ss(v.tensor_scalar(sc_sb[0:1, 1:2], tot_sb[0:1, 0:1], 1e-6, None, ALU.mult))
                        ss(v.scalar_tensor_tensor(sc_sb[0:1, 2:3], sc_sb[0:1, 0:1], 1.0 / 2047.0,
                                                  sc_sb[0:1, 1:2], ALU.mult, ALU.add))
                        ss(v.reciprocal(sc_sb[0:1, 3:4], sc_sb[0:1, 2:3]))
                        ss(v.tensor_tensor(sc_sb[0:1, 4:5], e2v[:], sc_sb[0:1, 3:4], ALU.mult))
                        ss(v.tensor_scalar(sc_sb[0:1, 5:6], sc_sb[0:1, 4:5], 1.0, 0.0, ALU.min, ALU.max))
                        ss(v.tensor_tensor(sc_sb[0:1, 6:7], bu_sb[0:1, bh:bh + 1],
                                           sc_sb[0:1, 5:6], ALU.is_lt))
                        ss(v.tensor_scalar(sc_sb[0:1, 7:8], sc_sb[0:1, 6:7], 1.0 / 2048.0, None, ALU.mult))
                        ss(v.tensor_scalar(ve_sb[:], v_sb[0:1, 16 * 128:17 * 128],
                                           sc_sb[0:1, 7:8], None, ALU.mult))
                        v.engine_nop().then_inc(s_dve, 1)
                    for qc in range(NQC):
                        for ki, K in enumerate(klists[qc]):
                            g = g_of(qc, ki)
                            j0 = j0f(qc, K)
                            jf = jff(qc, K)
                            v.wait_ge(s_act, expd(g))
                            if not l1:
                                if jf < j0:
                                    v.memset(et[g % 2][:, jf:j0], 0.0)
                                if K * 128 >= qc * 1024:
                                    v.tensor_tensor(et[g % 2][:, j0:j0 + 128],
                                                    et[g % 2][:, j0:j0 + 128],
                                                    tri_sb[:], ALU.mult)
                            else:
                                if K == 3:
                                    v.tensor_scalar(et[g % 2][:], et[g % 2][:],
                                                    cm3_sb[:, 0:1], None, ALU.mult)
                                if K == 16:
                                    v.tensor_copy(e2r[0:1, qc * 1024:(qc + 1) * 1024],
                                                  et[g % 2][0:1, :])
                            v.engine_nop().then_inc(s_msk, 1)
                        v.wait_ge(s_pd, pdd(qc, 1))
                        ss(v.reciprocal(rr[:], ps_s[1][0:1, 0:1024]))
                        v.engine_nop().then_inc(s_dve, 1)
                        if l1:
                            v.wait_ge(s_pd, pdd(qc, 2))
                            ss(v.tensor_tensor(s2u[:], ps_s[0][0:1, 0:1024],
                                               e2r[0:1, qc * 1024:(qc + 1) * 1024], ALU.subtract))
                            ss(v.tensor_copy(s2ubf[:], s2u[:]))
                            v.engine_nop().then_inc(s_dve, 1)
                        v.wait_ge(s_pd, pdd(qc, npd_qc))
                        ss(v.tensor_copy(rb_sb[:], ps_s[0][:]))
                        v.wait_ge(s_pv, pvd(qc, nks[qc] - 1) + (1 if l1 else 0))
                        gso = bh * 4 + qc
                        if gso >= 2:
                            v.wait_ge(s_so[gso % 2], 16 * ((gso - 2) // 2 + 1))
                        v.tensor_tensor(att[qc % 2][:], ps_pv[:], rb_sb[:],
                                        ALU.mult).then_inc(s_dve, 1)

                @blk.gpsimd
                def _(g_):
                    for qc in range(NQC):
                        if qc > 0:
                            g_.wait_ge(s_pd, pdd(qc - 1, 2 if l1 else 1))
                        g_.memset(acc[:], 0.0)
                        if l1:
                            g_.memset(accb[:], 0.0).then_inc(s_gp, 1)
                        else:
                            g_.memset(accb[0:1, 0:1], 0.0).then_inc(s_gp, 1)
                        for ki, K in enumerate(klists[qc]):
                            g = g_of(qc, ki)
                            jf = jff(qc, K)
                            g_.wait_ge(s_msk, mskd(g))
                            dst = accb if (l1 and K >= 16) else acc
                            g_.tensor_tensor(dst[:, jf:1024], dst[:, jf:1024],
                                             et[g % 2][:, jf:1024], ALU.add).then_inc(s_gp, 1)

                @blk.sync
                def _(sync):
                    for qc in range(NQC):
                        sync.wait_ge(s_dve, dved(qc, ndve_qc))
                        gso = bh * 4 + qc
                        sync.dma_start(out=aT_d[bh * 128:(bh + 1) * 128,
                                               qc * 1024:(qc + 1) * 1024],
                                       in_=att[qc % 2][:]).then_inc(s_so[gso % 2], 16)

            nc.all_engine_barrier()
            with nc.Block() as blk:
                @blk.vector
                def _(v):
                    for s in (s_ld, s_pe, s_pv, s_pd, s_act, s_msk, s_gp, s_dve, s_ss):
                        v.sem_clear(s)
            nc.all_engine_barrier()

        with nc.Block() as blk:
            @blk.sync
            def _(sync):
                sync.wait_ge(s_so[0], 16 * 2 * nbh)
                sync.wait_ge(s_so[1], 16 * 2 * nbh)
    return nc



NSLAB = 32


def _emit_deq(eng, src, ph, t2, sb2, bA, bB, bitsu, stile, wlo, whi):
    W = src.shape[1]
    eng.tensor_scalar(ph[:], src, 4, 15, ALU.logical_shift_right, ALU.bitwise_and)
    for cc, wdst in ((src, wlo), (ph[:], whi)):
        eng.tensor_scalar(t2[:], cc, 7, 6, ALU.bitwise_and, ALU.logical_shift_left)
        eng.tensor_scalar(sb2[:], cc, 8, 12, ALU.bitwise_and, ALU.logical_shift_left)
        eng.scalar_tensor_tensor(bA[:], t2[:], 16128, sb2[:], ALU.add, ALU.add)
        eng.scalar_tensor_tensor(bB[:], t2[:], 252, sb2[:], ALU.mult, ALU.add)
        eng.tensor_tensor(sb2[:], bA[:], bB[:], ALU.min)
        lowhalf = sb2[:].bitcast(BF16)
        bits_ap = bass.AP(lowhalf.tensor, lowhalf.offset, [[2 * W, 128], [2, W]])
        eng.tensor_tensor(wdst, bits_ap, stile, ALU.mult)


def build_l0ffn(nslab: int = NSLAB):
    nc = bass.Bass("TRN2", target_bir_lowering=False)
    x_d = nc.dram_tensor("x", [TOK, D], F32, kind="ExternalInput").ap()
    gpT_d = nc.dram_tensor("gpT", [1024, 16384], I32, kind="ExternalInput").ap()
    gsT_d = nc.dram_tensor("gsT", [64, 16384], F32, kind="ExternalInput").ap()
    upT_d = nc.dram_tensor("upT", [1024, 16384], I32, kind="ExternalInput").ap()
    usT_d = nc.dram_tensor("usT", [64, 16384], F32, kind="ExternalInput").ap()
    dpT_d = nc.dram_tensor("dpT", [8192, D], I32, kind="ExternalInput").ap()
    dsT_d = nc.dram_tensor("dsT", [512, D], F32, kind="ExternalInput").ap()
    xo_d = nc.dram_tensor("xo", [TOK, D], F32, kind="ExternalOutput").ap()
    nc._test_outs = {"xo": xo_d}

    ident = nc.inline_tensor(np.eye(128, dtype=bf), "ident128")
    C1 = 0.7978845608028654
    C3 = 0.7978845608028654 * 0.044715

    with ExitStack() as st_:
        def sb(name, shape, dt):
            return st_.enter_context(nc.sbuf_tensor(name, shape, dt))
        sums = sb("sums", [128, NTT], F32)
        sumsq = sb("sumsq", [128, NTT], F32)
        sumsq2 = sb("sumsq2", [128, NTT], F32)
        mneg = sb("mneg", [128, NTT], F32)
        varp = sb("varp", [128, NTT], F32)
        vinv = sb("vinv", [128, NTT], F32)
        rstd = sb("rstd", [128, NTT], F32)
        id_sb = sb("id_sb", [128, 128], BF16)
        yT = sb("yT", [128, NKC * TOK], BF16)
        gpk = [sb(f"gpk{i}", [128, 512], I32) for i in range(2)]
        upk = [sb(f"upk{i}", [128, 512], I32) for i in range(2)]
        gsk = [sb(f"gsk{i}", [128, 512], BF16) for i in range(2)]
        usk = [sb(f"usk{i}", [128, 512], BF16) for i in range(2)]
        dpk = sb("dpk", [128, 2048], I32)
        dsk = sb("dsk", [128, 2048], BF16)
        Wg = sb("Wg", [128, 16 * 512], BF16)
        Wu = sb("Wu", [128, 16 * 512], BF16)
        Wd = sb("Wd", [128, 4 * 2048], BF16)
        dqv = {nm: sb(f"dqv_{nm}", [128, 512], I32) for nm in ("ph", "t2", "sb2", "bA", "bB")}
        dqv["bitsu"] = sb("dqv_bits", [128, 512], U16)
        sqb = [sb(f"sqb{i}", [128, 1024], F32) for i in range(2)]
        fBb = [sb(f"fBb{i}", [128, 1024], F32) for i in range(2)]
        inter = sb("inter", [128, 4 * 2048], BF16)
        dtile = [sb(f"dtile{i}", [128, D], F32) for i in range(2)]
        xt = dtile
        yt = [sqb[i][:].bitcast(BF16) for i in range(2)]

        def yt_ap(i, off):
            v = sqb[i][:].bitcast(BF16)
            return bass.AP(v.tensor, v.offset + off, [[1024 * 2, 128], [2, 128]])

        ps_all = st_.enter_context(nc.psum_tensor("ps_all", [128, 4096], F32))
        pst = [ps_all[:, i * 1024:(i + 1) * 1024].bitcast(BF16) for i in range(2)]

        s_x = [st_.enter_context(nc.semaphore(f"s_x{i}")) for i in range(2)]
        s_g = [st_.enter_context(nc.semaphore(f"s_g{i}")) for i in range(2)]
        s_gd = st_.enter_context(nc.semaphore("s_gd"))
        s_c = st_.enter_context(nc.semaphore("s_c"))
        s_dve = st_.enter_context(nc.semaphore("s_dve"))
        s_act = st_.enter_context(nc.semaphore("s_act"))
        s_pe = st_.enter_context(nc.semaphore("s_pe"))
        s_ev = st_.enter_context(nc.semaphore("s_ev"))
        s_ss = st_.enter_context(nc.semaphore("s_ss"))
        s_dq = st_.enter_context(nc.semaphore("s_dq"))
        s_do = st_.enter_context(nc.semaphore("s_do"))

        with nc.Block() as blk:
            @blk.sync
            def _(sync):
                sync.dma_start(out=id_sb[:], in_=ident.ap()).then_inc(s_c, 16)
                sync.wait_ge(s_c, 16)
            @blk.gpsimd
            def _(g):
                g.dma_start(out=xo_d, in_=x_d).then_inc(s_do, 16)
                g.wait_ge(s_do, 16)
        nc.all_engine_barrier()

        with nc.Block() as blk:
            @blk.sync
            def _(sync):
                for tt in range(NTT):
                    if tt >= 2:
                        sync.wait_ge(s_dve, 2 * (tt - 1))
                    sync.dma_start(out=xt[tt % 2][:],
                                   in_=x_d[tt * 128:(tt + 1) * 128, :]).then_inc(s_x[tt % 2], 16)
            @blk.scalar
            def _(s):
                for tt in range(NTT):
                    s.wait_ge(s_x[tt % 2], 16 * (tt // 2 + 1))
                    s.activation(fBb[0][:], xt[tt % 2][:, 0:1024], AF.Square,
                                 accum_out=sumsq[:, tt:tt + 1])
                    s.activation(fBb[1][:], xt[tt % 2][:, 1024:2048], AF.Square,
                                 accum_out=sumsq2[:, tt:tt + 1]).then_inc(s_act, 1)
                    s.wait_ge(s_dve, 2 * tt + 1)
                    s.activation(rstd[:, tt:tt + 1], vinv[:, tt:tt + 1], AF.Sqrt).then_inc(s_act, 1)
            @blk.vector
            def _(v):
                rc = [0]
                def ss(ins):
                    rc[0] += 1
                    ins.then_inc(s_ss, 1)
                    v.wait_ge(s_ss, rc[0])
                for tt in range(NTT):
                    v.wait_ge(s_x[tt % 2], 16 * (tt // 2 + 1))
                    ss(v.reduce_sum(sums[:, tt:tt + 1], xt[tt % 2][:], axis=AX.X))
                    v.wait_ge(s_act, 2 * tt + 1)
                    ss(v.tensor_scalar(mneg[:, tt:tt + 1], sums[:, tt:tt + 1], -1.0 / D, None, ALU.mult))
                    ss(v.tensor_tensor(varp[:, tt:tt + 1], mneg[:, tt:tt + 1], mneg[:, tt:tt + 1], ALU.mult))
                    ss(v.tensor_tensor(rstd[:, tt:tt + 1], sumsq[:, tt:tt + 1], sumsq2[:, tt:tt + 1], ALU.add))
                    ss(v.scalar_tensor_tensor(varp[:, tt:tt + 1], rstd[:, tt:tt + 1], 1.0 / D,
                                              varp[:, tt:tt + 1], ALU.mult, ALU.subtract))
                    ss(v.tensor_scalar(rstd[:, tt:tt + 1], varp[:, tt:tt + 1], 1e-5, None, ALU.add))
                    ss(v.reciprocal(vinv[:, tt:tt + 1], rstd[:, tt:tt + 1]))
                    v.engine_nop().then_inc(s_dve, 1)
                    v.wait_ge(s_act, 2 * tt + 2)
                    if tt >= 2:
                        v.wait_ge(s_pe, tt - 1)
                    v.tensor_scalar(yt[tt % 2], xt[tt % 2][:], mneg[:, tt:tt + 1],
                                    rstd[:, tt:tt + 1], ALU.add, ALU.mult).then_inc(s_dve, 1)
                    v.wait_ge(s_pe, tt + 1)
                    v.tensor_copy(
                        bass.AP(yT, tt * 128, [[NKC * TOK, 128], [TOK, NKC], [1, 128]]),
                        pst[tt % 2].rearrange("p (c e) -> p c e", c=NKC),
                    ).then_inc(s_ev, 1)
            @blk.tensor
            def _(t):
                for tt in range(NTT):
                    t.wait_ge(s_dve, 2 * tt + 2)
                    if tt >= 2:
                        t.wait_ge(s_ev, tt - 1)
                    for c in range(NKC):
                        a, par = c // 2, c % 2
                        ins = t.transpose(pst[tt % 2][:, c * 128:(c + 1) * 128],
                                          yt_ap(tt % 2, 256 * a + par),
                                          id_sb[:])
                    ins.then_inc(s_pe, 1)
        nc.all_engine_barrier()
        c_pe = NTT
        c_act = 2 * NTT
        c_dve = 2 * NTT
        c_dq = 0
        c_g = [0, 0]
        c_gd = 0
        c_do = 16
        c_ss = [6 * NTT]

        for s in range(nslab):
            dq0 = c_dq
            g0 = list(c_g)
            gd0 = c_gd
            with nc.Block() as blk:
                @blk.gpsimd
                def _(g_):
                    for jt in range(8):
                        if jt >= 2:
                            g_.wait_ge(s_dq, dq0 + (jt - 2) + 1)
                        par = jt % 2
                        g_.dma_start(out=gpk[par][:],
                                     in_=bass.AP(gpT_d.tensor, jt * 128 * 16384 + s * 512,
                                                 [[16384, 128], [1, 512]])).then_inc(s_g[par], 16)
                        g_.dma_start(out=gsk[par][:],
                                     in_=bass.AP(gsT_d.tensor, (8 * jt) * 16384 + s * 512,
                                                 [[16384, 8], [0, 16], [1, 512]])).then_inc(s_g[par], 16)
                        g_.dma_start(out=upk[par][:],
                                     in_=bass.AP(upT_d.tensor, jt * 128 * 16384 + s * 512,
                                                 [[16384, 128], [1, 512]])).then_inc(s_g[par], 16)
                        g_.dma_start(out=usk[par][:],
                                     in_=bass.AP(usT_d.tensor, (8 * jt) * 16384 + s * 512,
                                                 [[16384, 8], [0, 16], [1, 512]])).then_inc(s_g[par], 16)
                        c_g[par] += 64
                    for a in range(2):
                        jt = 2 * s + a
                        if a == 1:
                            g_.wait_ge(s_dq, dq0 + 9)
                        g_.dma_start(out=dpk[:],
                                     in_=bass.AP(dpT_d.tensor, jt * 128 * 2048,
                                                 [[2048, 128], [1, 2048]])).then_inc(s_gd, 16)
                        g_.dma_start(out=dsk[:],
                                     in_=bass.AP(dsT_d.tensor, (8 * jt) * 2048,
                                                 [[2048, 8], [0, 16], [1, 2048]])).then_inc(s_gd, 16)

                @blk.vector
                def _(v):
                    for jt in range(8):
                        par = jt % 2
                        v.wait_ge(s_g[par], g0[par] + 64 * (jt // 2 + 1))
                        _emit_deq(v, gpk[par][:], dqv["ph"], dqv["t2"], dqv["sb2"],
                                  dqv["bA"], dqv["bB"], dqv["bitsu"], gsk[par][:],
                                  Wg[:, (2 * jt) * 512:(2 * jt + 1) * 512],
                                  Wg[:, (2 * jt + 1) * 512:(2 * jt + 2) * 512])
                        _emit_deq(v, upk[par][:], dqv["ph"], dqv["t2"], dqv["sb2"],
                                  dqv["bA"], dqv["bB"], dqv["bitsu"], usk[par][:],
                                  Wu[:, (2 * jt) * 512:(2 * jt + 1) * 512],
                                  Wu[:, (2 * jt + 1) * 512:(2 * jt + 2) * 512])
                        v.engine_nop().then_inc(s_dq, 1)
                    for a in range(2):
                        v.wait_ge(s_gd, gd0 + 32 * (a + 1))
                        for ch in range(4):
                            _emit_deq(v, dpk[:, ch * 512:(ch + 1) * 512],
                                      dqv["ph"], dqv["t2"], dqv["sb2"], dqv["bA"], dqv["bB"],
                                      dqv["bitsu"], dsk[:, ch * 512:(ch + 1) * 512],
                                      Wd[:, (2 * a) * 2048 + ch * 512:(2 * a) * 2048 + (ch + 1) * 512],
                                      Wd[:, (2 * a + 1) * 2048 + ch * 512:(2 * a + 1) * 2048 + (ch + 1) * 512])
                        v.engine_nop().then_inc(s_dq, 1)
            c_dq += 10
            c_gd += 64
            nc.all_engine_barrier()

            pe0, act0, dve0, do0 = c_pe, c_act, c_dve, c_do
            with nc.Block() as blk:
                @blk.tensor
                def _(t):
                    for u in range(8):
                        nt, th = u // 2, u % 2
                        par = u % 2
                        if u >= 2:
                            t.wait_ge(s_dve, dve0 + 2 * (u - 2) + 2)
                        ps_g = ps_all[:, par * 2048: par * 2048 + 1024]
                        ps_u = ps_all[:, par * 2048 + 1024: par * 2048 + 2048]
                        for dst, W in ((ps_g, Wg), (ps_u, Wu)):
                            for c in range(16):
                                for i in range(2):
                                    ins = t.matmul(dst[:, i * 512:(i + 1) * 512],
                                                   W[:, c * 512 + nt * 128: c * 512 + (nt + 1) * 128],
                                                   yT[:, c * TOK + th * 1024 + i * 512: c * TOK + th * 1024 + (i + 1) * 512],
                                                   start=(c == 0), stop=(c == 15))
                        ins.then_inc(s_pe, 1)
                    t.wait_ge(s_dve, dve0 + 16)
                    for tt in range(NTT):
                        par = tt % 2
                        if tt >= 2:
                            t.wait_ge(s_act, act0 + 16 + (tt - 2) + 1)
                        ps_d = ps_all[:, par * 2048: par * 2048 + 2048]
                        for c4 in range(4):
                            for hch in range(4):
                                ins = t.matmul(ps_d[:, hch * 512:(hch + 1) * 512],
                                               inter[:, c4 * 2048 + tt * 128: c4 * 2048 + (tt + 1) * 128],
                                               Wd[:, c4 * 2048 + hch * 512: c4 * 2048 + (hch + 1) * 512],
                                               start=(c4 == 0), stop=(c4 == 3))
                        ins.then_inc(s_pe, 1)

                @blk.scalar
                def _(s_):
                    for u in range(8):
                        par = u % 2
                        ps_g = ps_all[:, par * 2048: par * 2048 + 1024]
                        s_.wait_ge(s_pe, pe0 + u + 1)
                        if u >= 2:
                            s_.wait_ge(s_dve, dve0 + 2 * (u - 2) + 2)
                        s_.activation(sqb[par][:], ps_g, AF.Square).then_inc(s_act, 1)
                        s_.wait_ge(s_dve, dve0 + 2 * u + 1)
                        s_.activation(sqb[par][:], fBb[par][:], AF.Tanh).then_inc(s_act, 1)
                    for tt in range(NTT):
                        par = tt % 2
                        ps_d = ps_all[:, par * 2048: par * 2048 + 2048]
                        s_.wait_ge(s_pe, pe0 + 8 + tt + 1)
                        if tt >= 2:
                            s_.wait_ge(s_do, do0 + 16 * (tt - 2 + 1))
                        s_.activation(dtile[par][:], ps_d, AF.Copy).then_inc(s_act, 1)

                @blk.vector
                def _(v):
                    for u in range(8):
                        nt, th = u // 2, u % 2
                        par = u % 2
                        ps_g = ps_all[:, par * 2048: par * 2048 + 1024]
                        ps_u = ps_all[:, par * 2048 + 1024: par * 2048 + 2048]
                        v.wait_ge(s_act, act0 + 2 * u + 1)
                        v.tensor_scalar(sqb[par][:], sqb[par][:], C3, C1, ALU.mult, ALU.add)
                        v.tensor_tensor(fBb[par][:], sqb[par][:], ps_g,
                                        ALU.mult).then_inc(s_dve, 1)
                        v.wait_ge(s_act, act0 + 2 * u + 2)
                        v.scalar_tensor_tensor(fBb[par][:], sqb[par][:], 1.0, ps_g,
                                               ALU.add, ALU.mult)
                        v.scalar_tensor_tensor(
                            inter[:, nt * 2048 + th * 1024: nt * 2048 + (th + 1) * 1024],
                            fBb[par][:], 0.5, ps_u, ALU.mult, ALU.mult).then_inc(s_dve, 1)

                @blk.gpsimd
                def _(g_):
                    for tt in range(NTT):
                        g_.wait_ge(s_act, act0 + 16 + tt + 1)
                        g_.dma_start(out=xo_d[tt * 128:(tt + 1) * 128, :],
                                     in_=dtile[tt % 2][:], accum_op=ALU.add).then_inc(s_do, 16)
            c_pe += 24
            c_act += 32
            c_dve += 16
            c_do += 16 * 16
            nc.all_engine_barrier()

        with nc.Block() as blk:
            @blk.sync
            def _(sync):
                pass
            @blk.gpsimd
            def _(g_):
                g_.wait_ge(s_do, c_do)
    return nc



def _run(name, builder, builder_args, in_maps):
    if name not in _prog_cache:
        _prog_cache[name] = builder(*builder_args)
    nc = _prog_cache[name]
    res = run_bass_kernel_spmd(nc, in_maps, core_ids=list(range(NCORE)))
    return res.results


def rope_tables(parity):
    hd = 128
    inv_freq = 1.0 / (10000.0 ** (np.arange(0, hd, 2, dtype=np.float64) / hd))
    pos = np.arange(parity * TOK, (parity + 1) * TOK, dtype=np.float64)
    fr = pos[:, None] * inv_freq[None, :]
    return np.cos(fr).astype(np.float32), np.sin(fr).astype(np.float32)


def run_qkv_layer(x_full, wT, b, rope):
    in_maps = []
    for c in range(NCORE):
        m = {"x": np.ascontiguousarray(x_full[c * TOK:(c + 1) * TOK]), "wT": wT}
        if b is not None:
            m["b"] = np.ascontiguousarray(b.reshape(1, 3 * D))
        if rope:
            cs, sn = rope_tables(c % 2)
            m["cos"] = cs
            m["sin"] = sn
        in_maps.append(m)
    res = _run(f"qkv_rope{rope}_bias{b is not None}", build_qkv,
               (rope, b is not None), in_maps)
    q = np.concatenate([res[c]["q"] for c in range(NCORE)], 0)
    k = np.concatenate([res[c]["k"] for c in range(NCORE)], 0)
    v = np.concatenate([res[c]["v"] for c in range(NCORE)], 0)
    return q, k, v


def heads_of_core(c):
    return list(range(8 * (c % 2), 8 * (c % 2) + 8))


def attn_core_inputs(q, k, v, c, bern=None):
    b = c // 2
    hs = heads_of_core(c)
    qr = q.reshape(4, 4096, 16, 128)
    kr = k.reshape(4, 4096, 16, 128)
    vr = v.reshape(4, 4096, 16, 128)
    qT = np.ascontiguousarray(qr[b][:, hs, :].transpose(1, 2, 0).reshape(1024, 4096))
    kT = np.ascontiguousarray(kr[b][:, hs, :].transpose(1, 2, 0).reshape(1024, 4096))
    vv = np.ascontiguousarray(vr[b][:, hs, :].transpose(1, 0, 2).reshape(8 * 4096, 128))
    m = {"qT": qT, "kT": kT, "vv": vv}
    if bern is not None:
        m["bu"] = np.ascontiguousarray(bern[0, hs].reshape(1, 8).astype(np.float32))
    return m


def run_attn_layer(q, k, v, l1, bern=None):
    in_maps = [attn_core_inputs(q, k, v, c, bern if l1 else None) for c in range(NCORE)]
    res = _run(f"attn_l1{l1}", build_attn, (l1, 8), in_maps)
    attn = np.empty((4, 4096, 16, 128), dtype=bf)
    for c in range(NCORE):
        b = c // 2
        aT = res[c]["aT"].reshape(8, 128, 4096)
        for s, h in enumerate(heads_of_core(c)):
            attn[b, :, h, :] = aT[s].T
    return attn.reshape(16384, 2048)



NISLAB = 14


def build_moe_a():
    nc = bass.Bass("TRN2", target_bir_lowering=False)
    x_d = nc.dram_tensor("x", [TOK, D], F32, kind="ExternalInput").ap()
    cwT_d = nc.dram_tensor("cwT", [D, 8], F32, kind="ExternalInput").ap()
    cb_d = nc.dram_tensor("cb", [1, 8], F32, kind="ExternalInput").ap()
    wl_d = nc.dram_tensor("wl", [1, 8], F32, kind="ExternalInput").ap()
    bg_d = nc.dram_tensor("bg", [D, 7168], F32, kind="ExternalInput").ap()
    bu_d = nc.dram_tensor("bu_", [D, 7168], F32, kind="ExternalInput").ap()
    gA_d = nc.dram_tensor("gA", [D, 512], F32, kind="ExternalInput").ap()
    uA_d = nc.dram_tensor("uA", [D, 512], F32, kind="ExternalInput").ap()
    gB_d = nc.dram_tensor("gB", [8 * 64, 7168], F32, kind="ExternalInput").ap()
    uB_d = nc.dram_tensor("uB", [8 * 64, 7168], F32, kind="ExternalInput").ap()
    dA_d = nc.dram_tensor("dA", [7168, 512], F32, kind="ExternalInput").ap()
    sh_d = nc.dram_tensor("sumhw", [7168, TOK], BF16, kind="ExternalOutput").ap()
    za_d = nc.dram_tensor("zacc", [512, TOK], F32, kind="ExternalOutput").ap()
    nc._test_outs = {"sumhw": sh_d, "zacc": za_d}

    webc_d = nc.dram_tensor("webc", [8 * 128, TOK], BF16).ap()
    agT_d = nc.dram_tensor("agT", [64, 8 * TOK], BF16).ap()
    auT_d = nc.dram_tensor("auT", [64, 8 * TOK], BF16).ap()

    ident = nc.inline_tensor(np.eye(128, dtype=bf), "ident128")
    onesb = nc.inline_tensor(np.ones((1, 128), dtype=bf), "onesbrow")

    with ExitStack() as st_:
        def sb(name, shape, dt):
            return st_.enter_context(nc.sbuf_tensor(name, shape, dt))
        yT = sb("yT", [128, NKC * TOK], BF16)
        bg_sb = sb("bg_sb", [128, 16 * 512], BF16)
        bu_sb = sb("bu_sb", [128, 16 * 512], BF16)
        bgT = sb("bgT", [128, 4 * TOK], BF16)
        buT = sb("buT", [128, 4 * TOK], BF16)
        accs = sb("accs", [128, 4 * TOK], F32)
        scr = sb("scr", [128, 5 * TOK], BF16)
        webcs = sb("webcs", [128, TOK], BF16)
        agTb = sb("agTb", [64, TOK], BF16)
        auTb = sb("auTb", [64, TOK], BF16)
        gBb = sb("gBb", [64, 512], BF16)
        uBb = sb("uBb", [64, 512], BF16)
        da_sb = sb("da_sb", [128, 4 * 512], BF16)
        ztmp = sb("ztmp", [64, TOK], F32)
        weT = ztmp[:].bitcast(BF16)[0:8, 0:TOK]
        sums = sb("sums", [128, NTT], F32)
        sumsq = sb("sumsq", [128, NTT], F32)
        sumsq2 = sb("sumsq2", [128, NTT], F32)
        mneg = sb("mneg", [128, NTT], F32)
        varp = sb("varp", [128, NTT], F32)
        vinv = sb("vinv", [128, NTT], F32)
        rstd = sb("rstd", [128, NTT], F32)
        id_sb = sb("id_sb", [128, 128], BF16)
        ones_sb = sb("ones_sb", [1, 128], BF16)
        cwT_sb = sb("cwT_sb", [128, 16 * 8], BF16)
        cbrow = sb("cbrow", [1, 8], BF16)
        wbc = sb("wbc", [128, 8], F32)
        cf = sb("cf", [128, 8], F32)
        bids = sb("bids", [128, 8], F32)
        bids2 = sb("bids2", [128, 8], F32)
        m1 = sb("m1", [128, 8], F32)
        m2 = sb("m2", [128, 8], F32)
        t1c = sb("t1c", [128, 1], F32)
        t2c = sb("t2c", [128, 1], F32)
        e2c = sb("e2c", [128, 1], F32)
        r1c = sb("r1c", [128, 1], F32)
        r2c = sb("r2c", [128, 1], F32)
        webf = sb("webf", [128, 8], BF16)
        werow = agTb[0:1, 0:TOK]

        xt = [bgT[:, 0:4096].bitcast(F32), bgT[:, 4096:8192].bitcast(F32)]
        yt = [buT[:, 0:2048], buT[:, 2048:4096]]
        sqs = [accs[:, 0:1024], accs[:, 1024:2048]]

        ps_a = st_.enter_context(nc.psum_tensor("ps_a", [128, 2048], F32))
        ps_b = st_.enter_context(nc.psum_tensor("ps_b", [128, 2048], F32))
        pst = [ps_a[:, i * 1024:(i + 1) * 1024].bitcast(BF16) for i in range(2)]

        s_x = [st_.enter_context(nc.semaphore(f"s_x{i}")) for i in range(2)]
        s_g = [st_.enter_context(nc.semaphore(f"s_g{i}")) for i in range(2)]
        s_c = st_.enter_context(nc.semaphore("s_c"))
        s_dve = st_.enter_context(nc.semaphore("s_dve"))
        s_act = st_.enter_context(nc.semaphore("s_act"))
        s_pe = st_.enter_context(nc.semaphore("s_pe"))
        s_ev = st_.enter_context(nc.semaphore("s_ev"))
        s_ss = st_.enter_context(nc.semaphore("s_ss"))
        s_gp = st_.enter_context(nc.semaphore("s_gp"))
        s_do = st_.enter_context(nc.semaphore("s_do"))

        with nc.Block() as blk:
            @blk.sync
            def _(sync):
                sync.dma_start(out=id_sb[:], in_=ident.ap()).then_inc(s_c, 16)
                sync.dma_start(out=ones_sb[:], in_=onesb.ap()).then_inc(s_c, 16)
                sync.wait_ge(s_c, 32)
            @blk.gpsimd
            def _(g_):
                g_.dma_start(out=cwT_sb[:].rearrange("p (c w) -> p c w", c=16),
                             in_=cwT_d.rearrange("(c p) w -> p c w", p=128)).then_inc(s_g[0], 16)
                g_.dma_start(out=cbrow[:], in_=cb_d).then_inc(s_g[0], 16)
                g_.wait_ge(s_g[0], 32)
        nc.all_engine_barrier()

        with nc.Block() as blk:
            @blk.sync
            def _(sync):
                for tt in range(NTT):
                    if tt >= 2:
                        sync.wait_ge(s_dve, 2 * (tt - 1))
                    sync.dma_start(out=xt[tt % 2],
                                   in_=x_d[tt * 128:(tt + 1) * 128, :]).then_inc(s_x[tt % 2], 16)
            @blk.scalar
            def _(s):
                for tt in range(NTT):
                    s.wait_ge(s_x[tt % 2], 16 * (tt // 2 + 1))
                    s.activation(sqs[0], bass.AP(xt[tt % 2].tensor, xt[tt % 2].offset,
                                                 [[4 * TOK // 2, 128], [1, 1024]]),
                                 AF.Square, accum_out=sumsq[:, tt:tt + 1])
                    s.activation(sqs[1], bass.AP(xt[tt % 2].tensor, xt[tt % 2].offset + 1024,
                                                 [[4 * TOK // 2, 128], [1, 1024]]),
                                 AF.Square, accum_out=sumsq2[:, tt:tt + 1]).then_inc(s_act, 1)
                    s.wait_ge(s_dve, 2 * tt + 1)
                    s.activation(rstd[:, tt:tt + 1], vinv[:, tt:tt + 1], AF.Sqrt).then_inc(s_act, 1)
            @blk.vector
            def _(v):
                rc = [0]
                def ss(ins):
                    rc[0] += 1
                    ins.then_inc(s_ss, 1)
                    v.wait_ge(s_ss, rc[0])
                for tt in range(NTT):
                    v.wait_ge(s_x[tt % 2], 16 * (tt // 2 + 1))
                    ss(v.reduce_sum(sums[:, tt:tt + 1], xt[tt % 2], axis=AX.X))
                    v.wait_ge(s_act, 2 * tt + 1)
                    ss(v.tensor_scalar(mneg[:, tt:tt + 1], sums[:, tt:tt + 1], -1.0 / D, None, ALU.mult))
                    ss(v.tensor_tensor(varp[:, tt:tt + 1], mneg[:, tt:tt + 1], mneg[:, tt:tt + 1], ALU.mult))
                    ss(v.tensor_tensor(rstd[:, tt:tt + 1], sumsq[:, tt:tt + 1], sumsq2[:, tt:tt + 1], ALU.add))
                    ss(v.scalar_tensor_tensor(varp[:, tt:tt + 1], rstd[:, tt:tt + 1], 1.0 / D,
                                              varp[:, tt:tt + 1], ALU.mult, ALU.subtract))
                    ss(v.tensor_scalar(rstd[:, tt:tt + 1], varp[:, tt:tt + 1], 1e-5, None, ALU.add))
                    ss(v.reciprocal(vinv[:, tt:tt + 1], rstd[:, tt:tt + 1]))
                    v.engine_nop().then_inc(s_dve, 1)
                    v.wait_ge(s_act, 2 * tt + 2)
                    if tt >= 2:
                        v.wait_ge(s_pe, tt - 1)
                    v.tensor_scalar(yt[tt % 2], xt[tt % 2], mneg[:, tt:tt + 1],
                                    rstd[:, tt:tt + 1], ALU.add, ALU.mult).then_inc(s_dve, 1)
                    v.wait_ge(s_pe, tt + 1)
                    v.tensor_copy(
                        bass.AP(yT, tt * 128, [[NKC * TOK, 128], [TOK, NKC], [1, 128]]),
                        pst[tt % 2].rearrange("p (c e) -> p c e", c=NKC),
                    ).then_inc(s_ev, 1)
            @blk.tensor
            def _(t):
                for tt in range(NTT):
                    t.wait_ge(s_dve, 2 * tt + 2)
                    if tt >= 2:
                        t.wait_ge(s_ev, tt - 1)
                    for c in range(NKC):
                        ins = t.transpose(pst[tt % 2][:, c * 128:(c + 1) * 128],
                                          bass.AP(yt[tt % 2].tensor, yt[tt % 2].offset + c * 128,
                                                  [[4 * TOK, 128], [1, 128]]),
                                          id_sb[:])
                    ins.then_inc(s_pe, 1)
        nc.all_engine_barrier()
        s_tk = st_.enter_context(nc.semaphore("s_tk"))
        ops = []

        def op(eng, fn, amt=1):
            ops.append((eng, fn, amt))

        SCALING = 0.25

        op("gpsimd", lambda g_: g_.dma_start(out=cbrow[:], in_=cb_d), 16)
        wlrow = cbrow
        op("gpsimd", lambda g_: g_.dma_start(out=webf[0:1, 0:8], in_=wl_d), 16)
        op("tensor", lambda t: t.matmul(ps_b[:, 0:8], ones_sb[:], webf[0:1, 0:8],
                                        start=True, stop=True))
        op("vector", lambda v: v.tensor_copy(wbc[:], ps_b[:, 0:8]))
        for tt in range(NTT):
            def conf_mm(t, tt=tt):
                for c in range(NKC):
                    t.matmul(ps_a[:, 0:8], yT[:, c * TOK + tt * 128: c * TOK + (tt + 1) * 128],
                             cwT_sb[:, c * 8:(c + 1) * 8], start=(c == 0), stop=False)
                return t.matmul(ps_a[:, 0:8], ones_sb[:], cbrow[0:1, 0:8],
                                start=False, stop=True)
            op("tensor", conf_mm)
            op("scalar", lambda s, tt=tt: s.activation(cf[:], ps_a[:, 0:8], AF.Sigmoid))
            op("vector", lambda v: v.tensor_tensor(bids[:], cf[:], wbc[:], ALU.mult))
            op("vector", lambda v: v.reduce_max(t1c[:], bids[:], axis=AX.X))
            op("vector", lambda v: v.tensor_single_scalar(m1[:], bids[:], t1c[:, 0:1], ALU.is_equal))
            op("vector", lambda v: v.scalar_tensor_tensor(bids2[:], m1[:], -1e30, bids[:], ALU.mult, ALU.add))
            op("vector", lambda v: v.reduce_max(t2c[:], bids2[:], axis=AX.X))
            op("vector", lambda v: v.tensor_single_scalar(m2[:], bids2[:], t2c[:, 0:1], ALU.is_equal))
            op("vector", lambda v: v.tensor_tensor(e2c[:], t2c[:], t1c[:], ALU.subtract))
            op("scalar", lambda s: s.activation(r2c[:], e2c[:], AF.Exp))
            op("vector", lambda v: v.tensor_scalar(t1c[:], r2c[:], 1.0, None, ALU.add))
            op("vector", lambda v: v.reciprocal(r1c[:], t1c[:]))
            op("vector", lambda v: v.tensor_scalar(bids[:], m1[:], r1c[:, 0:1], None, ALU.mult))
            op("vector", lambda v: v.tensor_tensor(t2c[:], r2c[:], r1c[:], ALU.mult))
            op("vector", lambda v: v.scalar_tensor_tensor(
                bids[:], m2[:], t2c[:, 0:1], bids[:], ALU.mult, ALU.add))
            op("vector", lambda v: v.tensor_copy(webf[:], bids[:]))
            psT8 = ps_b[:, 1024:1088].bitcast(BF16)
            op("tensor", lambda t, tt=tt: t.transpose(psT8[0:8, 0:128], webf[:, 0:8], id_sb[:]))
            op("vector", lambda v, tt=tt: v.tensor_copy(weT[:, tt * 128:(tt + 1) * 128],
                                                        psT8[0:8, 0:128]))
        for e in range(8):
            op("gpsimd", lambda g_, e=e: g_.dma_start(out=werow, in_=weT[e:e + 1, :]), 16)
            def wemm(t, e=e):
                for sc in range(4):
                    ins = t.matmul(ps_a[:, sc * 512:(sc + 1) * 512], ones_sb[:],
                                   werow[:, sc * 512:(sc + 1) * 512], start=True, stop=True)
                return ins
            op("tensor", wemm)
            op("scalar", lambda s: s.activation(webcs[:], ps_a[:], AF.Copy))
            op("sync", lambda sy, e=e: sy.dma_start(out=webc_d[e * 128:(e + 1) * 128, :],
                                                    in_=webcs[:]), 16)

        for (A_d, T_d) in ((gA_d, agT_d), (uA_d, auT_d)):
            def lda(g_, A_d=A_d):
                return g_.dma_start(out=bg_sb[:].rearrange("p (c w) -> p c w", c=16),
                                    in_=bass.AP(A_d.tensor, 0, [[512, 128], [128 * 512, 16], [1, 512]]))
            op("gpsimd", lda, 16)
            for tt in range(NTT):
                def agmm(t, tt=tt):
                    for c in range(NKC):
                        ins = t.matmul(ps_a[:, 0:512],
                                       yT[:, c * TOK + tt * 128: c * TOK + (tt + 1) * 128],
                                       bg_sb[:, c * 512:(c + 1) * 512],
                                       start=(c == 0), stop=(c == NKC - 1))
                    return ins
                op("tensor", agmm)
                op("scalar", lambda s: s.activation(scr[:, 0:512], ps_a[:, 0:512],
                                                    AF.Copy, scale=SCALING))
                psT8b = ps_b[:, 1024:1152].bitcast(BF16)
                for e in range(8):
                    op("tensor", lambda t, e=e: t.transpose(
                        psT8b[0:64, 0:128], scr[:, e * 64:(e + 1) * 64], id_sb[:]))
                    op("vector", lambda v, e=e: v.tensor_copy(
                        werow[0:1, 0:128] if False else agTb[0:64, e * 128:(e + 1) * 128],
                        psT8b[0:64, 0:128]))
                def sta(sy, tt=tt, T_d=T_d):
                    return sy.dma_start(
                        out=bass.AP(T_d.tensor, tt * 128, [[8 * TOK, 64], [TOK, 8], [1, 128]]),
                        in_=agTb[0:64, 0:1024].rearrange("p (e t) -> p e t", e=8))
                op("sync", sta, 16)

        for ks in range(NISLAB):
            def ldslab(g_, ks=ks):
                g_.dma_start(out=bg_sb[:].rearrange("p (c w) -> p c w", c=16),
                             in_=bass.AP(bg_d.tensor, ks * 512,
                                         [[7168, 128], [128 * 7168, 16], [1, 512]])).then_inc(s_tk, 16)
                g_.dma_start(out=bu_sb[:].rearrange("p (c w) -> p c w", c=16),
                             in_=bass.AP(bu_d.tensor, ks * 512,
                                         [[7168, 128], [128 * 7168, 16], [1, 512]])).then_inc(s_tk, 16)
                g_.dma_start(out=da_sb[:].rearrange("p (c w) -> p c w", c=4),
                             in_=bass.AP(dA_d.tensor, ks * 512 * 512,
                                         [[512, 128], [128 * 512, 4], [1, 512]])).then_inc(s_tk, 16)
                return None
            op("gpsimd", ldslab, 48)
            for it in range(4):
                for (dst, W) in ((bgT, bg_sb), (buT, bu_sb)):
                    def bmm(t, it=it, W=W):
                        for c in range(NKC):
                            for i2 in range(4):
                                ins = t.matmul(ps_a[:, i2 * 512:(i2 + 1) * 512],
                                               W[:, c * 512 + it * 128: c * 512 + (it + 1) * 128],
                                               yT[:, c * TOK + i2 * 512: c * TOK + (i2 + 1) * 512],
                                               start=(c == 0), stop=(c == NKC - 1))
                        return ins
                    op("tensor", bmm)
                    op("scalar", lambda s, it=it, dst=dst: s.activation(
                        dst[:, it * TOK:(it + 1) * TOK], ps_a[:], AF.Copy))
            op("gpsimd", lambda g_: g_.memset(accs[:], 0.0))
            for e in range(8):
                def lde(g_, ks=ks, e=e):
                    g_.dma_start(out=gBb[:], in_=bass.AP(gB_d.tensor, (e * 64) * 7168 + ks * 512,
                                                         [[7168, 64], [1, 512]])).then_inc(s_tk, 16)
                    g_.dma_start(out=uBb[:], in_=bass.AP(uB_d.tensor, (e * 64) * 7168 + ks * 512,
                                                         [[7168, 64], [1, 512]])).then_inc(s_tk, 16)
                    g_.dma_start(out=webcs[:], in_=webc_d[e * 128:(e + 1) * 128, :]).then_inc(s_tk, 16)
                    g_.dma_start(out=agTb[0:64, 0:TOK], in_=agT_d[0:64, e * TOK:(e + 1) * TOK]).then_inc(s_tk, 16)
                    g_.dma_start(out=auTb[0:64, 0:TOK], in_=auT_d[0:64, e * TOK:(e + 1) * TOK]).then_inc(s_tk, 16)
                    return None
                op("gpsimd", lde, 80)
                for it in range(4):
                    def lgmm(t, it=it):
                        for sc in range(4):
                            ins = t.matmul(ps_a[:, sc * 512:(sc + 1) * 512],
                                           gBb[:, it * 128:(it + 1) * 128],
                                           agTb[0:64, sc * 512:(sc + 1) * 512],
                                           start=True, stop=True)
                        return ins
                    op("tensor", lgmm)
                    op("vector", lambda v, it=it: v.tensor_tensor(
                        scr[:, 0:TOK], ps_a[:], bgT[:, it * TOK:(it + 1) * TOK], ALU.add))
                    op("scalar", lambda s: s.activation(scr[:, TOK:2 * TOK], scr[:, 0:TOK], AF.Silu))
                    def lumm(t, it=it):
                        for sc in range(4):
                            ins = t.matmul(ps_a[:, sc * 512:(sc + 1) * 512],
                                           uBb[:, it * 128:(it + 1) * 128],
                                           auTb[0:64, sc * 512:(sc + 1) * 512],
                                           start=True, stop=True)
                        return ins
                    op("tensor", lumm)
                    op("vector", lambda v, it=it: v.tensor_tensor(
                        scr[:, 2 * TOK:3 * TOK], ps_a[:], buT[:, it * TOK:(it + 1) * TOK], ALU.add))
                    op("gpsimd", lambda g_: g_.tensor_tensor(
                        scr[:, 0:TOK], scr[:, TOK:2 * TOK], scr[:, 2 * TOK:3 * TOK], ALU.mult))
                    op("gpsimd", lambda g_: g_.tensor_tensor(
                        scr[:, 3 * TOK:4 * TOK], scr[:, 0:TOK], webcs[:], ALU.mult))
                    op("gpsimd", lambda g_, it=it: g_.tensor_tensor(
                        accs[:, it * TOK:(it + 1) * TOK], accs[:, it * TOK:(it + 1) * TOK],
                        scr[:, 3 * TOK:4 * TOK], ALU.add))
                    def zmm(t, it=it, e=e):
                        for sc in range(4):
                            ins = t.matmul(ps_b[0:64, sc * 512:(sc + 1) * 512],
                                           da_sb[:, it * 512 + e * 64: it * 512 + (e + 1) * 64],
                                           scr[:, 3 * TOK + sc * 512: 3 * TOK + (sc + 1) * 512],
                                           start=(it == 0), stop=(it == 3))
                        return ins
                    op("tensor", zmm)
                op("scalar", lambda s: s.activation(ztmp[:], ps_b[0:64, :], AF.Copy))
                if ks == 0:
                    op("gpsimd", lambda g_, e=e: g_.dma_start(
                        out=za_d[e * 64:(e + 1) * 64, :], in_=ztmp[:]), 16)
                else:
                    op("gpsimd", lambda g_, e=e: g_.dma_start(
                        out=za_d[e * 64:(e + 1) * 64, :], in_=ztmp[:], accum_op=ALU.add), 16)
            for it in range(4):
                op("scalar", lambda s, it=it: s.activation(
                    bgT[:, it * TOK:(it + 1) * TOK], accs[:, it * TOK:(it + 1) * TOK], AF.Copy))
                op("sync", lambda sy, ks=ks, it=it: sy.dma_start(
                    out=sh_d[ks * 512 + it * 128: ks * 512 + (it + 1) * 128, :],
                    in_=bgT[:, it * TOK:(it + 1) * TOK]), 16)

        cum = []
        tot = 0
        for (_, _, amt) in ops:
            cum.append(tot)
            tot += amt
        with nc.Block() as blk:
            def mk(engname):
                def body(eng):
                    for i, (en, fn, amt) in enumerate(ops):
                        if en != engname:
                            continue
                        if cum[i] > 0:
                            eng.wait_ge(s_tk, cum[i])
                        ins = fn(eng)
                        if ins is not None:
                            ins.then_inc(s_tk, amt)
                return body
            blk.tensor(mk("tensor"))
            blk.vector(mk("vector"))
            blk.scalar(mk("scalar"))
            blk.gpsimd(mk("gpsimd"))
            blk.sync(mk("sync"))
        nc.all_engine_barrier()
        with nc.Block() as blk:
            @blk.sync
            def _(sync):
                sync.wait_ge(s_tk, tot)
    return nc



def build_moe_b():
    nc = bass.Bass("TRN2", target_bir_lowering=False)
    sh_d = nc.dram_tensor("sumhw", [7168, TOK], BF16, kind="ExternalInput").ap()
    za_d = nc.dram_tensor("zacc", [512, TOK], F32, kind="ExternalInput").ap()
    bd_d = nc.dram_tensor("bd", [7168, D], F32, kind="ExternalInput").ap()
    dB_d = nc.dram_tensor("dB", [512, D], F32, kind="ExternalInput").ap()
    x_d = nc.dram_tensor("x", [TOK, D], F32, kind="ExternalInput").ap()
    xo_d = nc.dram_tensor("xo", [TOK, D], F32, kind="ExternalOutput").ap()
    nc._test_outs = {"xo": xo_d}

    with ExitStack() as st_:
        def sb(name, shape, dt):
            return st_.enter_context(nc.sbuf_tensor(name, shape, dt))
        zb = sb("zb", [64, 8 * D], BF16)
        zs = sb("zs", [64, 8 * D], BF16)
        dBs = sb("dBs", [64, 8 * D], BF16)
        sh_sb = sb("sh_sb", [128, 4 * TOK], BF16)
        bd_sb = sb("bd_sb", [128, 4 * D], BF16)
        xt = sb("xt", [128, D], F32)
        ot = sb("ot", [128, D], F32)
        ps_a = st_.enter_context(nc.psum_tensor("ps_a", [128, 2048], F32))
        s_tk = st_.enter_context(nc.semaphore("s_tk"))

        ops = []
        def op(eng, fn, amt=1):
            ops.append((eng, fn, amt))

        op("gpsimd", lambda g_: g_.dma_start(
            out=zb[:].rearrange("p (e t) -> p e t", e=8),
            in_=bass.AP(za_d.tensor, 0, [[TOK, 64], [64 * TOK, 8], [1, TOK]])), 16)
        op("gpsimd", lambda g_: g_.dma_start(
            out=dBs[:].rearrange("p (e t) -> p e t", e=8),
            in_=bass.AP(dB_d.tensor, 0, [[D, 64], [64 * D, 8], [1, D]])), 16)
        op("vector", lambda v: v.tensor_scalar(zs[:], zb[:], 0.25, None, ALU.mult))
        for tt in range(NTT):
            op("sync", lambda sy, tt=tt: sy.dma_start(
                out=xt[:], in_=x_d[tt * 128:(tt + 1) * 128, :]), 16)
            def zpass(t, tt=tt):
                for e in range(8):
                    for sc in range(4):
                        ins = t.matmul(ps_a[:, sc * 512:(sc + 1) * 512],
                                       zs[:, e * D + tt * 128: e * D + (tt + 1) * 128],
                                       dBs[:, e * D + sc * 512: e * D + (sc + 1) * 512],
                                       start=(e == 0), stop=(e == 7))
                return ins
            op("tensor", zpass)
            op("vector", lambda v: v.tensor_tensor(ot[:], ps_a[:], xt[:], ALU.add))
            op("sync", lambda sy, tt=tt: sy.dma_start(
                out=xo_d[tt * 128:(tt + 1) * 128, :], in_=ot[:]), 16)
        for ks in range(NISLAB):
            def lds(g_, ks=ks):
                g_.dma_start(out=sh_sb[:].rearrange("p (c t) -> p c t", c=4),
                             in_=bass.AP(sh_d.tensor, ks * 512 * TOK,
                                         [[TOK, 128], [128 * TOK, 4], [1, TOK]])).then_inc(s_tk, 16)
                g_.dma_start(out=bd_sb[:].rearrange("p (c t) -> p c t", c=4),
                             in_=bass.AP(bd_d.tensor, ks * 512 * D,
                                         [[D, 128], [128 * D, 4], [1, D]])).then_inc(s_tk, 16)
                return None
            op("gpsimd", lds, 32)
            for tt in range(NTT):
                def dmm(t, tt=tt):
                    for it in range(4):
                        for hch in range(4):
                            ins = t.matmul(ps_a[:, hch * 512:(hch + 1) * 512],
                                           sh_sb[:, it * TOK + tt * 128: it * TOK + (tt + 1) * 128],
                                           bd_sb[:, it * D + hch * 512: it * D + (hch + 1) * 512],
                                           start=(it == 0), stop=(it == 3))
                    return ins
                op("tensor", dmm)
                op("scalar", lambda s: s.activation(ot[:], ps_a[:], AF.Copy))
                op("gpsimd", lambda g_, tt=tt: g_.dma_start(
                    out=xo_d[tt * 128:(tt + 1) * 128, :], in_=ot[:], accum_op=ALU.add), 16)

        cum = []
        tot = 0
        for (_, _, amt) in ops:
            cum.append(tot)
            tot += amt
        with nc.Block() as blk:
            def mk(engname):
                def body(eng):
                    for i, (en, fn, amt) in enumerate(ops):
                        if en != engname:
                            continue
                        if cum[i] > 0:
                            eng.wait_ge(s_tk, cum[i])
                        ins = fn(eng)
                        if ins is not None:
                            ins.then_inc(s_tk, amt)
                return body
            blk.tensor(mk("tensor"))
            blk.vector(mk("vector"))
            blk.scalar(mk("scalar"))
            blk.gpsimd(mk("gpsimd"))
            blk.sync(mk("sync"))
        nc.all_engine_barrier()
        with nc.Block() as blk:
            @blk.sync
            def _(sync):
                sync.wait_ge(s_tk, tot)
    return nc


PI_G = np.concatenate([
    np.concatenate([np.arange(256 * a, 256 * (a + 1), 2),
                    np.arange(256 * a + 1, 256 * (a + 1), 2)])
    for a in range(64)
])


def l0ffn_weight_maps(gp, gs, up, us, dp, ds):
    return {
        "gpT": np.ascontiguousarray(gp[PI_G].T),
        "gsT": np.ascontiguousarray(gs[PI_G].T),
        "upT": np.ascontiguousarray(up[PI_G].T),
        "usT": np.ascontiguousarray(us[PI_G].T),
        "dpT": np.ascontiguousarray(dp.T),
        "dsT": np.ascontiguousarray(ds.T),
    }


def run_proj_layer(attn, wT, b, x_full, bias):
    in_maps = []
    for c in range(NCORE):
        m = {"aT": np.ascontiguousarray(attn[c * TOK:(c + 1) * TOK, :].T),
             "wT": wT, "x": np.ascontiguousarray(x_full[c * TOK:(c + 1) * TOK])}
        if bias:
            m["b"] = np.ascontiguousarray(b.reshape(1, D))
        in_maps.append(m)
    res = _run(f"proj_bias{bias}", build_proj, (bias,), in_maps)
    return np.concatenate([res[c]["xo"] for c in range(NCORE)], 0)


def run_l0ffn_layer(x_full, wmaps):
    in_maps = []
    for c in range(NCORE):
        m = dict(wmaps)
        m["x"] = np.ascontiguousarray(x_full[c * TOK:(c + 1) * TOK])
        in_maps.append(m)
    res = _run("l0ffn", build_l0ffn, (), in_maps)
    return np.concatenate([res[c]["xo"] for c in range(NCORE)], 0)


def run_moe_layer(x_full, inputs):
    gA = np.ascontiguousarray(np.asarray(inputs["L1_ffn_gate_A"], np.float32)
                              .transpose(1, 0, 2).reshape(D, 512))
    uA = np.ascontiguousarray(np.asarray(inputs["L1_ffn_up_A"], np.float32)
                              .transpose(1, 0, 2).reshape(D, 512))
    dA = np.ascontiguousarray(np.asarray(inputs["L1_ffn_down_A"], np.float32)
                              .transpose(1, 0, 2).reshape(7168, 512))
    gB = np.ascontiguousarray(np.asarray(inputs["L1_ffn_gate_B"], np.float32).reshape(512, 7168))
    uB = np.ascontiguousarray(np.asarray(inputs["L1_ffn_up_B"], np.float32).reshape(512, 7168))
    dB = np.ascontiguousarray(np.asarray(inputs["L1_ffn_down_B"], np.float32).reshape(512, D))
    bg = np.ascontiguousarray(np.asarray(inputs["L1_ffn_base_gate"], np.float32))
    bu_ = np.ascontiguousarray(np.asarray(inputs["L1_ffn_base_up"], np.float32))
    bd = np.ascontiguousarray(np.asarray(inputs["L1_ffn_base_down"], np.float32))
    cwT = np.ascontiguousarray(np.asarray(inputs["L1_ffn_conf_W"], np.float32).T)
    cb = np.ascontiguousarray(np.asarray(inputs["L1_ffn_conf_b"], np.float32).reshape(1, 8))
    wl = np.ascontiguousarray(np.asarray(inputs["L1_ffn_expert_wealth"], np.float32).reshape(1, 8))

    in_maps = []
    for c in range(NCORE):
        m = {"x": np.ascontiguousarray(x_full[c * TOK:(c + 1) * TOK]),
             "cwT": cwT, "cb": cb, "wl": wl, "bg": bg, "bu_": bu_,
             "gA": gA, "uA": uA, "gB": gB, "uB": uB, "dA": dA}
        in_maps.append(m)
    res_a = _run("moe_a", build_moe_a, (), in_maps)

    in_maps = []
    for c in range(NCORE):
        m = {"sumhw": res_a[c]["sumhw"], "zacc": res_a[c]["zacc"],
             "bd": bd, "dB": dB,
             "x": np.ascontiguousarray(x_full[c * TOK:(c + 1) * TOK])}
        in_maps.append(m)
    res_b = _run("moe_b", build_moe_b, (), in_maps)
    return np.concatenate([res_b[c]["xo"] for c in range(NCORE)], 0)


def kernel(**inputs):
    x = np.ascontiguousarray(np.asarray(inputs["L0_attn_x"], np.float32).reshape(16384, D))

    WqkvT = np.ascontiguousarray(np.asarray(inputs["L0_attn_W_qkv"], np.float32).T)
    q, k, v = run_qkv_layer(x, WqkvT, np.asarray(inputs["L0_attn_b_qkv"], np.float32), rope=False)
    attn = run_attn_layer(q, k, v, l1=False)
    x1 = run_proj_layer(attn, np.ascontiguousarray(np.asarray(inputs["L0_attn_W_out"], np.float32).T),
                        np.asarray(inputs["L0_attn_b_out"], np.float32), x, bias=True)

    wmaps = l0ffn_weight_maps(
        np.asarray(inputs["L0_ffn_gate_W_packed"], np.int32),
        np.asarray(inputs["L0_ffn_gate_scales"], np.float32),
        np.asarray(inputs["L0_ffn_up_W_packed"], np.int32),
        np.asarray(inputs["L0_ffn_up_scales"], np.float32),
        np.asarray(inputs["L0_ffn_down_W_packed"], np.int32),
        np.asarray(inputs["L0_ffn_down_scales"], np.float32))
    x2 = run_l0ffn_layer(x1, wmaps)

    wT1 = np.ascontiguousarray(np.concatenate(
        [np.asarray(inputs["L1_attn_wq"], np.float32).T,
         np.asarray(inputs["L1_attn_wk"], np.float32).T,
         np.asarray(inputs["L1_attn_wv"], np.float32).T], axis=1))
    q, k, v = run_qkv_layer(x2, wT1, None, rope=True)
    bern = np.asarray(inputs["L1_attn_bern_u"], np.float32)
    attn = run_attn_layer(q, k, v, l1=True, bern=bern)
    x3 = run_proj_layer(attn, np.ascontiguousarray(np.asarray(inputs["L1_attn_wo"], np.float32).T),
                        None, x2, bias=False)

    x4 = run_moe_layer(x3, inputs)
    return np.ascontiguousarray(x4.reshape(4, 4096, D).astype(np.float32))
